# version 07
# baseline (speedup 1.0000x reference)
import numpy as np
import ml_dtypes
import concourse.bass as bass
import concourse.mybir as mybir
from concourse.bass_utils import run_bass_kernel_spmd

F32 = mybir.dt.float32
BF16 = mybir.dt.bfloat16
FP8 = mybir.dt.float8e4
AF = mybir.ActivationFunctionType
ALU = mybir.AluOpType
AXX = mybir.AxisListType.X

BF = ml_dtypes.bfloat16
D = 2048
DC = D // 128
TSEQ = 4096
B = 2
TC = 1024
HALO = 128
TT = TC + HALO
NTT = TT // 128
MID = 0.5009765625
TCS_T = [(0, 512), (512, 512), (1024, 128)]
TCS_C = [(0, 512), (512, 512)]

SCAN_M = 20
SCAN_C = 0.003
SCAN_L = 256

INTER = 4096
NHEAD = 64
PDIM = 64
NSTATE = 128
CONV_DIM = INTER + 2 * NSTATE
KCONV = 4
CH = 256



def tile_lhsT(w, pad_to=None):
    J, Dd = w.shape
    if pad_to and J < pad_to:
        w2 = np.zeros((pad_to, Dd), dtype=w.dtype)
        w2[:J] = w
        w = w2
        J = pad_to
    assert J % 128 == 0 and Dd % 128 == 0
    return np.ascontiguousarray(
        w.T.reshape(Dd // 128, 128, J // 128, 128).transpose(2, 1, 0, 3))


def colmajor(vec):
    return np.ascontiguousarray(vec.reshape(-1, 128).T)


def scan_consts():
    from math import comb
    L = SCAN_L
    ls = np.arange(L, dtype=np.float64)
    kap = np.zeros((SCAN_M + 1, L))
    for m in range(SCAN_M + 1):
        cmb = np.array([comb(int(l), m) for l in range(L)], dtype=np.float64)
        kap[m] = 0.5 ** ls * cmb * (2 * SCAN_C) ** m
    q = np.zeros(SCAN_M + 1)
    for m in range(SCAN_M + 1):
        mx = kap[m].max()
        q[m] = np.floor(np.log2(mx)) if mx > 0 else 0.0
    kap_hat = kap / 2.0 ** q[:, None]
    topl = np.zeros((2, SCAN_M + 1, 128, 128))
    for dd in range(2):
        for m in range(SCAN_M + 1):
            for tp in range(128):
                lo = tp - 128 * dd
                for t in range(128):
                    lag = t + 128 * dd - tp
                    if 0 <= lag < L:
                        topl[dd, m, tp, t] = kap_hat[m, lag]
    toplS = np.ascontiguousarray(topl.transpose(2, 0, 1, 3).reshape(128, 2 * (SCAN_M + 1), 128))
    return toplS.astype(BF), (2.0 ** q).astype(np.float64)



class Prog:
    ENGS = ("sync", "gpsimd", "vector", "scalar", "tensor")

    def __init__(self, nc, sems):
        self.nc = nc
        self.sem = sems
        self.q = {e: [] for e in self.ENGS}
        self.cnt = {e: 0 for e in self.ENGS}
        self.dmac = {"dmah": 0, "dmas": 0}

    def emit(self, eng, fn):
        self.q[eng].append(fn)

    def mark(self, eng):
        self.cnt[eng] += 1
        sem = self.sem[eng]
        if eng != "sync":
            self.emit(eng, lambda e: e.drain())
        self.emit(eng, lambda e: e.nop().then_inc(sem, 1))
        return self.cnt[eng]

    def wait(self, eng, other, val):
        if val <= 0:
            return
        sem = self.sem[other]
        self.emit(eng, lambda e: e.wait_ge(sem, val))

    def dma_on(self, eng, out_ap, in_ap):
        which = "dmas" if eng == "gpsimd" else "dmah"
        sem = self.sem[which]
        self.dmac[which] += 16
        v = self.dmac[which]
        self.emit(eng, lambda e, o=out_ap, i=in_ap: e.dma_start(out=o, in_=i).then_inc(sem, 16))
        self.emit(eng, lambda e: e.wait_ge(sem, v))
        return (which, v)

    def wait_dma(self, eng, tok):
        if tok == "all":
            for which in ("dmah", "dmas"):
                if self.dmac[which]:
                    sem = self.sem[which]
                    v = self.dmac[which]
                    self.emit(eng, lambda e, sem=sem, v=v: e.wait_ge(sem, v))
            return
        which, val = tok
        sem = self.sem[which]
        self.emit(eng, lambda e: e.wait_ge(sem, val))

    def vdrain(self):
        self.emit("vector", lambda v: v.drain())

    def run_blocks(self, block):
        for eng_name in self.ENGS:
            dec = getattr(block, eng_name)
            thunks = self.q[eng_name]

            def make(th):
                def body(eng):
                    for t in th:
                        t(eng)
                return body
            dec(make(thunks))


def prog_ctx(nc, ctx):
    sems = {nm: ctx.enter_context(nc.semaphore("s_" + nm))
            for nm in ("sync", "gpsimd", "vector", "scalar", "tensor", "dmah", "dmas")}
    return Prog(nc, sems)


def emit_ln(p, X, XL, sqt, stats, MEMb, INVb, psA, psB, ones1, T, tcs, ones_col=None):
    for ci, (t0, tw) in enumerate(tcs):
        for dc in range(DC):
            sm0 = p.cnt["scalar"]
            p.emit("scalar", lambda s, dc=dc, t0=t0, tw=tw: s.activation(
                out=sqt[:, 0:tw], in_=X[:, dc, t0:t0 + tw], func=AF.Square))
            sm = p.mark("scalar")
            p.wait("tensor", "scalar", sm)
            p.emit("tensor", lambda t, dc=dc, t0=t0, tw=tw: t.matmul(
                psA[0:1, 0:tw], ones_col[:], X[:, dc, t0:t0 + tw],
                start=(dc == 0), stop=(dc == DC - 1)))
            p.emit("tensor", lambda t, dc=dc, tw=tw: t.matmul(
                psB[0:1, 0:tw], ones_col[:], sqt[:, 0:tw],
                start=(dc == 0), stop=(dc == DC - 1)))
            tm = p.mark("tensor")
            p.wait("scalar", "tensor", tm)
        tm = p.cnt["tensor"]
        p.wait("vector", "tensor", tm)
        p.emit("vector", lambda v, t0=t0, tw=tw: v.tensor_copy(stats[:, 0, t0:t0 + tw], psA[0:1, 0:tw]))
        p.emit("vector", lambda v, t0=t0, tw=tw: v.tensor_copy(stats[:, 1, t0:t0 + tw], psB[0:1, 0:tw]))
        p.vdrain()
        vm = p.mark("vector")
        p.wait("tensor", "vector", vm)
    p.emit("vector", lambda v: v.tensor_scalar(out=stats[:, 0, :], in0=stats[:, 0, :],
                                               scalar1=1.0 / D, scalar2=None, op0=ALU.mult))
    p.vdrain()
    p.emit("vector", lambda v: v.tensor_tensor(stats[:, 2, :], stats[:, 0, :], stats[:, 0, :], op=ALU.mult))
    p.vdrain()
    p.emit("vector", lambda v: v.scalar_tensor_tensor(out=stats[:, 1, :], in0=stats[:, 1, :],
                                                      scalar=1.0 / D, in1=stats[:, 2, :],
                                                      op0=ALU.mult, op1=ALU.subtract))
    p.vdrain()
    p.emit("vector", lambda v: v.tensor_scalar(out=stats[:, 1, :], in0=stats[:, 1, :],
                                               scalar1=1e-5, scalar2=None, op0=ALU.add))
    p.vdrain()
    vm = p.mark("vector")
    p.wait("scalar", "vector", vm)
    p.emit("scalar", lambda s: s.activation(out=stats[:, 2, :], in_=stats[:, 1, :], func=AF.Sqrt))
    sm = p.mark("scalar")
    p.wait("vector", "scalar", sm)
    p.emit("vector", lambda v: v.reciprocal(stats[:, 1, :], stats[:, 2, :]))
    p.vdrain()
    vm = p.mark("vector")
    p.wait("tensor", "vector", vm)
    for t0, tw in tcs:
        p.emit("vector", lambda v, t0=t0, tw=tw: v.tensor_copy(sqt[0:1, 0:tw], stats[:, 1, t0:t0 + tw]))
        p.vdrain()
        vm = p.mark("vector")
        p.wait("tensor", "vector", vm)
        p.emit("tensor", lambda t, t0=t0, tw=tw: t.matmul(
            psA[:, 0:tw], ones1[:], stats[:, 0, t0:t0 + tw], start=True, stop=True))
        p.emit("tensor", lambda t, tw=tw: t.matmul(
            psB[:, 0:tw], ones1[:], sqt[0:1, 0:tw], start=True, stop=True))
        tm = p.mark("tensor")
        p.wait("vector", "tensor", tm)
        p.emit("vector", lambda v, t0=t0, tw=tw: v.tensor_copy(MEMb[:, t0:t0 + tw], psA[:, 0:tw]))
        p.emit("vector", lambda v, t0=t0, tw=tw: v.tensor_copy(INVb[:, t0:t0 + tw], psB[:, 0:tw]))
        p.vdrain()
        vm = p.mark("vector")
        p.wait("tensor", "vector", vm)
    for dc in range(DC):
        p.emit("vector", lambda v, dc=dc: v.tensor_tensor(XL[:, dc, :], X[:, dc, :], MEMb[:, 0:T], op=ALU.subtract))
    p.vdrain()
    for dc in range(DC):
        p.emit("vector", lambda v, dc=dc: v.tensor_tensor(XL[:, dc, :], XL[:, dc, :], INVb[:, 0:T], op=ALU.mult))
    p.vdrain()
    return p.mark("vector")


def emit_stream_proj(p, wsrc, wbuf, src, nj, tcs, psA, psB, src_ready_v, epilogue,
                     wsrc_is_bf16=False):
    base_t = p.cnt["tensor"]
    npc = len(tcs)
    for jt in range(nj):
        p.wait("gpsimd", "tensor", base_t + npc * (jt - 1) if jt >= 2 else base_t)
        eng = "sync" if wsrc_is_bf16 else "gpsimd"
        dv = p.dma_on(eng, wbuf[:, jt % 2, :, :], wsrc[jt])
        p.wait_dma("tensor", dv)
        p.wait("tensor", "vector", src_ready_v)
        for ci, (t0, tw) in enumerate(tcs):
            psum = (psA, psB)[(jt * npc + ci) % 2]
            for dc in range(DC):
                p.emit("tensor", lambda t, jt=jt, dc=dc, t0=t0, tw=tw, psum=psum: t.matmul(
                    psum[:, 0:tw], wbuf[:, jt % 2, dc, :], src[:, dc, t0:t0 + tw],
                    start=(dc == 0), stop=(dc == DC - 1)))
            tm = p.mark("tensor")
            p.wait("vector", "tensor", tm)
            epilogue(jt, ci, t0, tw, psum)
            vm = p.mark("vector")
            p.wait("tensor", "vector", vm)
    return p.cnt["vector"]


def emit_softplus(p, dst, src, tmp):
    p.emit("scalar", lambda s: s.activation(out=tmp, in_=src, func=AF.Abs))
    p.emit("scalar", lambda s: s.drain())
    p.emit("scalar", lambda s: s.activation(out=tmp, in_=tmp, func=AF.Exp, scale=-1.0))
    p.emit("scalar", lambda s: s.drain())
    p.emit("scalar", lambda s: s.activation(out=tmp, in_=tmp, func=AF.Ln, bias=1.0))
    sm = p.mark("scalar")
    p.wait("vector", "scalar", sm)
    p.emit("vector", lambda v: v.scalar_tensor_tensor(out=dst, in0=src, scalar=0.0,
                                                      in1=tmp, op0=ALU.max, op1=ALU.add))
    p.vdrain()
    return p.mark("vector")


def setup_const1(nc, ctx, p):
    c1 = ctx.enter_context(nc.sbuf_tensor("c_one", [128, 1], F32))
    p.emit("gpsimd", lambda g: g.memset(c1[:], 1.0))
    nc.const_aps.register(F32, 1.0, c1[:])
    return c1



def build_launch_A(upto=99):
    from contextlib import ExitStack
    nc = bass.Bass("TRN2", target_bir_lowering=False)
    NMS = (SCAN_M + 1) * 16
    qinv = scan_consts()[1]

    xin = nc.dram_tensor("xin", [DC, 128, TT], F32, kind="ExternalInput").ap()
    pin_t = nc.dram_tensor("pin_t", [DC, 128, DC, 128], F32, kind="ExternalInput").ap()
    pout_t = nc.dram_tensor("pout_t", [DC, 128, DC, 128], F32, kind="ExternalInput").ap()
    bcol_in = nc.dram_tensor("bcol_in", [128, DC], F32, kind="ExternalInput").ap()
    bcol_out = nc.dram_tensor("bcol_out", [128, DC], F32, kind="ExternalInput").ap()
    amatA = nc.dram_tensor("amatA", [128, DC, 16], F32, kind="ExternalInput").ap()
    bwT = nc.dram_tensor("bwT", [128, DC, 16], F32, kind="ExternalInput").ap()
    cwT = nc.dram_tensor("cwT", [16, DC, 128], F32, kind="ExternalInput").ap()
    dcolD = nc.dram_tensor("dcolD", [128, DC], F32, kind="ExternalInput").ap()
    topl = nc.dram_tensor("topl", [128, 2 * (SCAN_M + 1), 128], BF16, kind="ExternalInput").ap()
    xout = nc.dram_tensor("xout", [DC, 128, TC], F32, kind="ExternalOutput").ap()

    ctx = ExitStack()
    sb = lambda nm, sh, dt: ctx.enter_context(nc.sbuf_tensor(nm, sh, dt))
    psr = lambda nm, sh: ctx.enter_context(nc.psum_tensor(nm, sh, F32))

    X = sb("X", [128, DC, TT], F32)
    XL = sb("XL", [128, DC, TT], BF16)
    XP = sb("XP", [128, DC, TT], BF16)
    Y = XL
    eptmp = sb("eptmp", [128, 512], F32)
    Tt = sb("Tt", [128, NTT, NMS + 16], BF16)
    U = sb("U", [128, NTT, NMS], BF16)
    DLt = sb("DLt", [128, NTT, 16], F32)
    SPT = sb("SPT", [128, NTT, 16], F32)
    Ssb = sb("Ssb", [128, NTT, 16], F32)
    Sfm = sb("Sfm", [16, TT], BF16)
    WBIG = sb("WBIG", [128, 2 * DC * 180], BF16)
    wbuf = WBIG[:].rearrange("p (b c j) -> p b c j", b=2, j=180)[:, :, :, 0:128]
    sqt = sb("sqt", [128, 512], F32)
    MEMb = sb("MEMb", [128, TT], F32)
    INVb = sb("INVb", [128, TT], F32)
    cA = sb("cA", [128, DC * 16], F32)
    gA = sb("gA", [128, DC * 16], F32)
    gacc = sb("gacc", [128, DC * 16], F32)
    rhsA = WBIG[:, 0:DC * (NMS + 16)].rearrange("p (c s) -> p c s", s=NMS + 16)
    bwTs = sb("bwTs", [128, DC, 16], BF16)
    cwTs = sb("cwTs", [16, DC, 128], BF16)
    dcol = sb("dcol", [128, DC], F32)
    bci = sb("bci", [128, DC], F32)
    bco = sb("bco", [128, DC], F32)
    toplS = WBIG[:, 0:2 * (SCAN_M + 1) * 128].rearrange("p (a t) -> p a t", t=128)
    ones1 = sb("ones1", [1, 128], F32)
    ones_col = sb("ones_col", [128, 1], F32)
    ident = sb("ident", [128, 128], F32)

    psA = psr("psA", [128, 512])
    psB = psr("psB", [128, 512])
    psS = psr("psS", [128, 16])
    psTr = psr("psTr", [16, 128])

    p = prog_ctx(nc, ctx)
    setup_const1(nc, ctx, p)

    dv = 0
    for dc in range(DC):
        dv = p.dma_on("sync", X[:, dc, :], xin[dc])
    dv_in = dv
    for dst, srcd in [(cA[:].rearrange("p (c s) -> p c s", s=16), amatA), (bwTs[:], bwT),
                      (cwTs[:], cwT), (dcol[:], dcolD), (bci[:], bcol_in),
                      (bco[:], bcol_out)]:
        dv = p.dma_on("gpsimd", dst, srcd)
    dv_const = dv
    p.emit("gpsimd", lambda g: g.memset(ones1[:], 1.0))
    p.emit("gpsimd", lambda g: g.memset(ones_col[:], 1.0))
    p.emit("vector", lambda v: v.memset(ident[:], 0.0))
    p.vdrain()
    vmi = p.mark("vector")
    p.wait("gpsimd", "vector", vmi)
    p.emit("gpsimd", lambda g: g.affine_select(out=ident[:], in_=ident[:], compare_op=ALU.not_equal,
                                               fill=1.0, base=0, pattern=[[-1, 128]], channel_multiplier=1))
    gm0 = p.mark("gpsimd")

    if upto < 1:
        return _finishA(p, nc, ctx, X, xout, 0, "vector")
    p.wait_dma("scalar", dv_in)
    p.wait_dma("vector", dv_in)
    p.wait_dma("tensor", dv_in)
    p.wait("tensor", "gpsimd", gm0)
    p.wait("scalar", "gpsimd", gm0)
    stats = ctxstats(nc, ctx)
    vm_xl = emit_ln(p, X, XL, sqt, stats, MEMb, INVb, psA, psB, ones1, TT, TCS_T, ones_col)

    if upto < 2:
        return _finishA(p, nc, ctx, X, xout, vm_xl, "vector")
    def epi_bias(jt, ci, t0, tw, psum):
        p.emit("vector", lambda v: v.tensor_scalar(
            out=XP[:, jt, t0:t0 + tw], in0=psum[:, 0:tw],
            scalar1=bci[:, jt:jt + 1], scalar2=None, op0=ALU.add))
        p.vdrain()
    p.wait_dma("vector", dv_const)
    emit_stream_proj(p, pin_t, wbuf, XL, DC, TCS_T, psA, psB, vm_xl, epi_bias)

    if upto < 3:
        return _finishA(p, nc, ctx, X, xout, p.cnt["vector"], "vector")
    p.wait_dma("scalar", dv_const)
    p.emit("scalar", lambda s: s.activation(out=gA[:], in_=cA[:], func=AF.Sigmoid))
    sm = p.mark("scalar")
    p.wait("vector", "scalar", sm)
    p.emit("vector", lambda v: v.tensor_scalar(out=gA[:], in0=gA[:], scalar1=-0.5,
                                               scalar2=1.0 / SCAN_C, op0=ALU.add, op1=ALU.mult))
    p.emit("vector", lambda v: v.memset(gacc[:], 1.0))
    p.vdrain()
    gacc3 = gacc[:].rearrange("p (c s) -> p c s", s=16)
    for m in range(SCAN_M + 1):
        p.emit("vector", lambda v, m=m: v.tensor_scalar(
            out=rhsA[:, :, m * 16:(m + 1) * 16], in0=gacc3,
            scalar1=float(qinv[m]), scalar2=None, op0=ALU.mult))
        if m < SCAN_M:
            p.emit("vector", lambda v: v.tensor_tensor(gacc[:], gacc[:], gA[:], op=ALU.mult))
            p.vdrain()
    p.emit("vector", lambda v: v.tensor_copy(rhsA[:, :, NMS:], bwTs[:]))
    p.vdrain()
    vm_rhsA = p.mark("vector")

    if upto < 4:
        return _finishA(p, nc, ctx, X, xout, vm_rhsA, "vector")
    p.wait("tensor", "vector", vm_rhsA)
    for tt in range(NTT):
        for dc in range(DC):
            p.emit("tensor", lambda t, tt=tt, dc=dc: t.matmul(
                psA[:, 0:NMS + 16], XP[:, dc, tt * 128:(tt + 1) * 128], rhsA[:, dc, :],
                start=(dc == 0), stop=(dc == DC - 1)))
        tm = p.mark("tensor")
        p.wait("vector", "tensor", tm)
        p.emit("vector", lambda v, tt=tt: v.tensor_copy(Tt[:, tt, :], psA[:, 0:NMS + 16]))
        p.vdrain()
        vm = p.mark("vector")
        p.wait("tensor", "vector", vm)
    p.wait("scalar", "vector", p.cnt["vector"])
    emit_softplus(p, DLt[:], Tt[:, :, NMS:], SPT[:])
    for tt in range(NTT):
        p.emit("vector", lambda v, tt=tt: v.tensor_tensor(
            U[:, tt, :].rearrange("p (m s) -> p m s", s=16),
            Tt[:, tt, 0:NMS].rearrange("p (m s) -> p m s", s=16),
            DLt[:, tt, None, :].broadcast_to([128, SCAN_M + 1, 16]),
            op=ALU.mult))
    p.vdrain()
    vm_u = p.mark("vector")

    if upto < 5:
        return _finishA(p, nc, ctx, X, xout, vm_u, "vector")
    p.wait("gpsimd", "tensor", p.cnt["tensor"])
    dv_top = p.dma_on("gpsimd", toplS, topl)
    p.wait_dma("tensor", dv_top)
    p.wait("tensor", "vector", vm_u)
    for tt in range(NTT):
        first = True
        for delta in (1, 0):
            tp = tt - delta
            if tp < 0:
                continue
            for m in range(SCAN_M + 1):
                last = (delta == 0 and m == SCAN_M)
                p.emit("tensor", lambda t, tp=tp, delta=delta, m=m, first=first, last=last: t.matmul(
                    psS[:], toplS[:, delta * (SCAN_M + 1) + m, :], U[:, tp, m * 16:(m + 1) * 16],
                    start=first, stop=last))
                first = False
        tm = p.mark("tensor")
        p.wait("vector", "tensor", tm)
        p.emit("vector", lambda v, tt=tt: v.tensor_copy(Ssb[:, tt, :], psS[:]))
        p.vdrain()
        vm = p.mark("vector")
        p.wait("tensor", "vector", vm)
    for tt in range(NTT):
        p.emit("tensor", lambda t, tt=tt: t.transpose(psTr[:], Ssb[:, tt, :], ident[:]))
        tm = p.mark("tensor")
        p.wait("vector", "tensor", tm)
        p.emit("vector", lambda v, tt=tt: v.tensor_copy(Sfm[:, tt * 128:(tt + 1) * 128], psTr[:]))
        p.vdrain()
        vm = p.mark("vector")
        p.wait("tensor", "vector", vm)

    if upto < 6:
        return _finishA(p, nc, ctx, X, xout, p.cnt["vector"], "vector")
    p.wait_dma("tensor", dv_const)
    for et in range(DC):
        for (t0, tw) in TCS_T:
            p.emit("tensor", lambda t, et=et, t0=t0, tw=tw: t.matmul(
                psA[:, 0:tw], cwTs[:, et, :], Sfm[:, t0:t0 + tw], start=True, stop=True))
            tm = p.mark("tensor")
            p.wait("vector", "tensor", tm)
            p.emit("vector", lambda v, et=et, t0=t0, tw=tw: v.scalar_tensor_tensor(
                out=Y[:, et, t0:t0 + tw], in0=XP[:, et, t0:t0 + tw], scalar=dcol[:, et:et + 1],
                in1=psA[:, 0:tw], op0=ALU.mult, op1=ALU.add))
            p.vdrain()
            vm = p.mark("vector")
            p.wait("tensor", "vector", vm)
    vm_y = vm

    if upto < 7:
        return _finishA(p, nc, ctx, X, xout, vm_y, "vector")
    def epi_resid(jt, ci, t0, tw, psum):
        p.emit("vector", lambda v: v.tensor_scalar(
            out=psum[:, 0:tw], in0=psum[:, 0:tw],
            scalar1=bco[:, jt:jt + 1], scalar2=None, op0=ALU.add))
        p.emit("vector", lambda v: v.tensor_tensor(
            eptmp[:, 0:tw], X[:, jt, t0:t0 + tw], MEMb[:, t0:t0 + tw], op=ALU.subtract))
        p.vdrain()
        p.emit("vector", lambda v: v.tensor_tensor(
            eptmp[:, 0:tw], eptmp[:, 0:tw], INVb[:, t0:t0 + tw], op=ALU.mult))
        p.vdrain()
        p.emit("vector", lambda v: v.tensor_tensor(
            psum[:, 0:tw], psum[:, 0:tw], eptmp[:, 0:tw], op=ALU.add))
        p.vdrain()
        p.emit("vector", lambda v: v.tensor_tensor(
            X[:, jt, t0:t0 + tw], X[:, jt, t0:t0 + tw], psum[:, 0:tw], op=ALU.add))
        p.vdrain()
    vm_fin = emit_stream_proj(p, pout_t, wbuf, Y, DC, TCS_T, psA, psB, vm_y, epi_resid)

    p.wait("sync", "vector", vm_fin)
    for dc in range(DC):
        dv = p.dma_on("sync", xout[dc], X[:, dc, HALO:])
    p.wait_dma("gpsimd", dv)

    with ctx, nc.Block() as block:
        p.run_blocks(block)
    return nc


def _finishA(p, nc, ctx, X, xout, vm, eng):
    p.wait("sync", eng, vm)
    p.wait_dma("sync", "all")
    dv = 0
    for dc in range(DC):
        dv = p.dma_on("sync", xout[dc], X[:, dc, HALO:])
    p.wait_dma("gpsimd", dv)
    with ctx, nc.Block() as block:
        p.run_blocks(block)
    return nc


def host_inputs_A(inputs):
    toplS, qinv = scan_consts()
    return {
        "pin_t": tile_lhsT(np.asarray(inputs["L0_attn_proj_in_w"], np.float32)),
        "pout_t": tile_lhsT(np.asarray(inputs["L0_attn_proj_out_w"], np.float32)),
        "bcol_in": colmajor(np.asarray(inputs["L0_attn_proj_in_b"], np.float32)),
        "bcol_out": colmajor(np.asarray(inputs["L0_attn_proj_out_b"], np.float32)),
        "amatA": np.ascontiguousarray(np.asarray(inputs["L0_attn_A"], np.float32)
                                      .reshape(DC, 128, 16).transpose(1, 0, 2)),
        "bwT": np.ascontiguousarray(np.asarray(inputs["L0_attn_B_w"], np.float32)
                                    .T.reshape(DC, 128, 16).transpose(1, 0, 2)),
        "cwT": np.ascontiguousarray(np.asarray(inputs["L0_attn_C_w"], np.float32)
                                    .T.reshape(16, DC, 128)),
        "dcolD": colmajor(np.asarray(inputs["L0_attn_D_param"], np.float32)),
        "topl": toplS,
    }


def slice_cores_fm(x_fm, halo):
    out = []
    W = TC + halo
    for b in range(B):
        for c in range(4):
            s0 = c * TC
            sl = np.zeros((D, W), dtype=x_fm.dtype)
            lo = max(0, s0 - halo)
            sl[:, W - TC - (s0 - lo):] = x_fm[:, b * TSEQ + lo: b * TSEQ + s0 + TC]
            out.append(np.ascontiguousarray(sl.reshape(DC, 128, W)))
    return out





QROWS = 8320


def build_launch_Q(nrow_tern, nrow_mx):
    from contextlib import ExitStack
    nc = bass.Bass("TRN2", target_bir_lowering=False)
    wt_in = nc.dram_tensor("wt_in", [nrow_tern, 2048], F32, kind="ExternalInput").ap()
    wm_in = nc.dram_tensor("wm_in", [nrow_mx, 2048], F32, kind="ExternalInput").ap()
    wt_out = nc.dram_tensor("wt_out", [nrow_tern, 2048], BF16, kind="ExternalOutput").ap()
    wm_out = nc.dram_tensor("wm_out", [nrow_mx, 2048], BF16, kind="ExternalOutput").ap()

    ctx = ExitStack()
    sb = lambda nm, sh, dt: ctx.enter_context(nc.sbuf_tensor(nm, sh, dt))
    W = sb("W", [128, 2048], BF16)
    Q8 = sb("Q8", [128, 2048], FP8)
    OUT = sb("OUT", [128, 2048], BF16)
    RED = sb("RED", [128, 64], F32)
    SCL = sb("SCL", [128, 64], F32)
    SCB = sb("SCB", [128, 32], BF16)
    TH = sb("TH", [128, 32], F32)
    T1 = sb("T1", [128, 2048], BF16)
    T2 = sb("T2", [128, 2048], BF16)
    p = prog_ctx(nc, ctx)

    def tern_tile(r):
        p.wait("gpsimd", "vector", p.cnt["vector"])
        dv = p.dma_on("gpsimd", W[:], wt_in[r:r + 128, :])
        p.wait_dma("vector", dv)
        p.emit("vector", lambda v: v.reduce_sum(RED[:, 0:32], W[:].rearrange("p (g e) -> p g e", e=64),
                                                axis=AXX, apply_absolute_value=True))
        p.vdrain()
        p.emit("vector", lambda v: v.tensor_scalar(out=SCB[:], in0=RED[:, 0:32], scalar1=1.0 / 64,
                                                   scalar2=1e-8, op0=ALU.mult, op1=ALU.max))
        p.vdrain()
        p.emit("vector", lambda v: v.tensor_scalar(out=TH[:], in0=SCB[:], scalar1=MID,
                                                   scalar2=None, op0=ALU.mult))
        p.vdrain()
        p.emit("vector", lambda v: v.tensor_tensor(
            T1[:].rearrange("p (g e) -> p g e", e=64), W[:].rearrange("p (g e) -> p g e", e=64),
            TH[:, :, None].broadcast_to([128, 32, 64]), op=ALU.is_gt))
        p.emit("vector", lambda v: v.tensor_scalar(out=TH[:], in0=TH[:], scalar1=-1.0,
                                                   scalar2=None, op0=ALU.mult))
        p.vdrain()
        p.emit("vector", lambda v: v.tensor_tensor(
            T2[:].rearrange("p (g e) -> p g e", e=64), W[:].rearrange("p (g e) -> p g e", e=64),
            TH[:, :, None].broadcast_to([128, 32, 64]), op=ALU.is_lt))
        p.vdrain()
        p.emit("vector", lambda v: v.tensor_tensor(T1[:], T1[:], T2[:], op=ALU.subtract))
        p.vdrain()
        p.emit("vector", lambda v: v.tensor_tensor(
            OUT[:].rearrange("p (g e) -> p g e", e=64), T1[:].rearrange("p (g e) -> p g e", e=64),
            SCB[:, :, None].broadcast_to([128, 32, 64]), op=ALU.mult))
        p.vdrain()
        vm = p.mark("vector")
        p.wait("sync", "vector", vm)
        dvo = p.dma_on("sync", wt_out[r:r + 128, :], OUT[:])
        p.wait_dma("vector", dvo)
        return dvo

    def mx_tile(r):
        p.wait("gpsimd", "vector", p.cnt["vector"])
        dv = p.dma_on("gpsimd", W[:], wm_in[r:r + 128, :])
        p.wait_dma("vector", dv)
        p.emit("vector", lambda v: v.reduce_max(RED[:], W[:].rearrange("p (g e) -> p g e", e=32),
                                                axis=AXX, apply_absolute_value=True))
        p.vdrain()
        p.emit("vector", lambda v: v.tensor_scalar(out=SCL[:], in0=RED[:], scalar1=1.0 / 448.0,
                                                   scalar2=1e-30, op0=ALU.mult, op1=ALU.max))
        p.vdrain()
        p.emit("vector", lambda v: v.reciprocal(RED[:], SCL[:]))
        p.vdrain()
        p.emit("vector", lambda v: v.tensor_scalar(out=RED[:], in0=RED[:], scalar1=0.5,
                                                   scalar2=None, op0=ALU.mult))
        p.emit("vector", lambda v: v.tensor_scalar(out=SCL[:], in0=SCL[:], scalar1=2.0,
                                                   scalar2=None, op0=ALU.mult))
        p.vdrain()
        p.emit("vector", lambda v: v.tensor_tensor(
            Q8[:].rearrange("p (g e) -> p g e", e=32), W[:].rearrange("p (g e) -> p g e", e=32),
            RED[:, :, None].broadcast_to([128, 64, 32]), op=ALU.mult))
        p.vdrain()
        p.emit("vector", lambda v: v.tensor_tensor(
            OUT[:].rearrange("p (g e) -> p g e", e=32), Q8[:].rearrange("p (g e) -> p g e", e=32),
            SCL[:, :, None].broadcast_to([128, 64, 32]), op=ALU.mult))
        p.vdrain()
        vm = p.mark("vector")
        p.wait("sync", "vector", vm)
        dvo = p.dma_on("sync", wm_out[r:r + 128, :], OUT[:])
        p.wait_dma("vector", dvo)
        return dvo

    last = None
    for r in range(0, nrow_tern, 128):
        last = tern_tile(r)
    for r in range(0, nrow_mx, 128):
        last = mx_tile(r)
    p.wait_dma("gpsimd", "all")
    with ctx, nc.Block() as block:
        p.run_blocks(block)
    return nc



def build_launch_B():
    from contextlib import ExitStack
    nc = bass.Bass("TRN2", target_bir_lowering=False)
    JH = 8192
    NJT = JH // 128
    xin = nc.dram_tensor("xin", [DC, 128, TC], F32, kind="ExternalInput").ap()
    wgu_t = nc.dram_tensor("wgu_t", [2 * NJT, 128, DC, 128], BF16, kind="ExternalInput").ap()
    wp_t = nc.dram_tensor("wp_t", [DC, 128, JH // 128, 128], BF16, kind="ExternalInput").ap()
    xout = nc.dram_tensor("xout", [DC, 128, TC], F32, kind="ExternalOutput").ap()

    ctx = ExitStack()
    sb = lambda nm, sh, dt: ctx.enter_context(nc.sbuf_tensor(nm, sh, dt))
    psr = lambda nm, sh: ctx.enter_context(nc.psum_tensor(nm, sh, F32))
    X = sb("X", [128, DC, TC], F32)
    XL = sb("XL", [128, DC, TC], BF16)
    H = sb("H", [128, JH // 128, 512], BF16)
    wbuf = sb("wbuf", [128, 4, DC, 128], BF16)
    sqt = sb("sqt", [128, 512], F32)
    stats = sb("stats", [1, 3, TC], F32)
    MEMb = sb("MEMb", [128, TC], F32)
    INVb = sb("INVb", [128, TC], F32)
    RINVb = sb("RINVb", [128, TC], F32)
    silu_t = sb("silu_t", [128, 512], F32)
    ones1 = sb("ones1", [1, 128], F32)
    onesb = sb("onesb", [1, 128], BF16)
    ones_col = sb("ones_col", [128, 1], F32)
    onesb_col = sb("onesb_col", [128, 1], BF16)
    psA = psr("psA", [128, 512])
    psB = psr("psB", [128, 512])
    psC = psr("psC", [128, 512])
    psD = psr("psD", [128, 512])

    p = prog_ctx(nc, ctx)
    setup_const1(nc, ctx, p)

    dv = 0
    for dc in range(DC):
        dv = p.dma_on("sync", X[:, dc, :], xin[dc])
    dv_in = dv
    p.emit("gpsimd", lambda g: g.memset(ones1[:], 1.0))
    p.emit("gpsimd", lambda g: g.memset(ones_col[:], 1.0))
    p.emit("gpsimd", lambda g: g.memset(onesb[:], 1.0))
    p.emit("gpsimd", lambda g: g.memset(onesb_col[:], 1.0))
    gm0 = p.mark("gpsimd")
    p.wait_dma("scalar", dv_in)
    p.wait_dma("vector", dv_in)
    p.wait_dma("tensor", dv_in)
    p.wait("tensor", "gpsimd", gm0)
    p.wait("scalar", "gpsimd", gm0)
    vm_xl = emit_ln(p, X, XL, sqt, stats, MEMb, INVb, psA, psB, ones1, TC, TCS_C, ones_col)

    for h0 in (0, 512):
        base_t = p.cnt["tensor"]
        p.wait("tensor", "vector", p.cnt["vector"])
        for jt in range(NJT):
            p.wait("gpsimd", "tensor", base_t + 2 * (jt - 1) if jt >= 2 else base_t)
            dvg = p.dma_on("gpsimd", wbuf[:, 2 * (jt % 2), :, :], wgu_t[jt])
            dvu = p.dma_on("gpsimd", wbuf[:, 2 * (jt % 2) + 1, :, :], wgu_t[NJT + jt])
            p.wait_dma("tensor", dvu)
            for dc in range(DC):
                p.emit("tensor", lambda t, jt=jt, dc=dc, h0=h0: t.matmul(
                    psA[:, 0:512], wbuf[:, 2 * (jt % 2), dc, :], XL[:, dc, h0:h0 + 512],
                    start=(dc == 0), stop=(dc == DC - 1)))
            for dc in range(DC):
                p.emit("tensor", lambda t, jt=jt, dc=dc, h0=h0: t.matmul(
                    psC[:, 0:512], wbuf[:, 2 * (jt % 2) + 1, dc, :], XL[:, dc, h0:h0 + 512],
                    start=(dc == 0), stop=(dc == DC - 1)))
            tm = p.mark("tensor")
            p.wait("scalar", "tensor", tm)
            p.emit("scalar", lambda s: s.activation(out=silu_t[:, 0:512], in_=psA[:, 0:512], func=AF.Silu))
            sm = p.mark("scalar")
            p.wait("vector", "scalar", sm)
            p.emit("vector", lambda v, jt=jt: v.tensor_tensor(
                H[:, jt, :], silu_t[:, 0:512], psC[:, 0:512], op=ALU.mult))
            p.vdrain()
            vm = p.mark("vector")
            p.wait("tensor", "vector", vm)
            tm = p.mark("tensor")
        vm_h = p.cnt["vector"]
        p.wait("scalar", "vector", vm_h)
        for hc in range(NJT):
            p.emit("scalar", lambda s, hc=hc: s.activation(out=sqt[:, 0:512], in_=H[:, hc, :], func=AF.Square))
            sm = p.mark("scalar")
            p.wait("tensor", "scalar", sm)
            p.emit("tensor", lambda t, hc=hc: t.matmul(
                psA[0:1, 0:512], ones_col[:], sqt[:, 0:512],
                start=(hc == 0), stop=(hc == NJT - 1)))
            tm = p.mark("tensor")
            p.wait("scalar", "tensor", tm)
        p.wait("vector", "tensor", p.cnt["tensor"])
        p.emit("vector", lambda v, h0=h0: v.tensor_scalar(
            out=stats[:, 0, h0:h0 + 512], in0=psA[0:1, 0:512],
            scalar1=1.0 / JH, scalar2=1.1920928955078125e-07, op0=ALU.mult, op1=ALU.add))
        p.vdrain()
        vm = p.mark("vector")
        p.wait("scalar", "vector", vm)
        p.emit("scalar", lambda s, h0=h0: s.activation(out=stats[:, 1, h0:h0 + 512],
                                                       in_=stats[:, 0, h0:h0 + 512], func=AF.Sqrt))
        sm = p.mark("scalar")
        p.wait("vector", "scalar", sm)
        p.emit("vector", lambda v, h0=h0: v.reciprocal(stats[:, 2, h0:h0 + 512], stats[:, 1, h0:h0 + 512]))
        p.vdrain()
        vm = p.mark("vector")
        p.wait("tensor", "vector", vm)
        p.emit("vector", lambda v, h0=h0: v.tensor_copy(sqt[0:1, 0:512], stats[:, 2, h0:h0 + 512]))
        p.vdrain()
        vmq = p.mark("vector")
        p.wait("tensor", "vector", vmq)
        p.emit("tensor", lambda t: t.matmul(
            psA[:, 0:512], ones1[:], sqt[0:1, 0:512], start=True, stop=True))
        tm = p.mark("tensor")
        p.wait("vector", "tensor", tm)
        p.emit("vector", lambda v: v.tensor_copy(RINVb[:, 0:512], psA[:, 0:512]))
        p.vdrain()
        for hc in range(NJT):
            p.emit("vector", lambda v, hc=hc: v.tensor_tensor(H[:, hc, :], H[:, hc, :],
                                                              RINVb[:, 0:512], op=ALU.mult))
        p.vdrain()
        vm_hn = p.mark("vector")
        base_t2 = p.cnt["tensor"]
        p.wait("tensor", "vector", vm_hn)
        for jt in range(DC):
            p.wait("gpsimd", "tensor", base_t2 + jt if jt >= 1 else base_t2)
            dvw = p.dma_on("gpsimd", wbuf[:].rearrange("p a c j -> p (a c) j"), wp_t[jt])
            p.wait_dma("tensor", dvw)
            for hc in range(NJT):
                p.emit("tensor", lambda t, jt=jt, hc=hc: t.matmul(
                    psA[:, 0:512], wbuf[:].rearrange("p a c j -> p (a c) j")[:, hc, :],
                    H[:, hc, :], start=(hc == 0), stop=(hc == NJT - 1)))
            tm = p.mark("tensor")
            p.wait("vector", "tensor", tm)
            p.emit("vector", lambda v, jt=jt, h0=h0: v.tensor_tensor(
                X[:, jt, h0:h0 + 512], X[:, jt, h0:h0 + 512], psA[:, 0:512], op=ALU.add))
            p.vdrain()
            vm = p.mark("vector")
            p.wait("tensor", "vector", vm)
    vm_fin = p.cnt["vector"]
    p.wait("sync", "vector", vm_fin)
    p.wait_dma("sync", "all")
    for dc in range(DC):
        dv = p.dma_on("sync", xout[dc], X[:, dc, :])
    p.wait_dma("gpsimd", "all")
    with ctx, nc.Block() as block:
        p.run_blocks(block)
    return nc



def build_launch_D():
    from contextlib import ExitStack
    nc = bass.Bass("TRN2", target_bir_lowering=False)
    JH = 8192
    NJT = JH // 128
    xin = nc.dram_tensor("xin", [DC, 128, TC], F32, kind="ExternalInput").ap()
    wgu_t = nc.dram_tensor("wgu_t", [2 * NJT, 128, DC, 128], BF16, kind="ExternalInput").ap()
    wp_t = nc.dram_tensor("wp_t", [DC, 128, NJT, 128], BF16, kind="ExternalInput").ap()
    xout = nc.dram_tensor("xout", [DC, 128, TC], F32, kind="ExternalOutput").ap()

    ctx = ExitStack()
    sb = lambda nm, sh, dt: ctx.enter_context(nc.sbuf_tensor(nm, sh, dt))
    X = sb("X", [128, DC, TC], F32)
    XL = sb("XL", [128, DC, TC], BF16)
    H = sb("H", [128, JH // 128, 512], BF16)
    wbuf = sb("wbuf", [128, 4, DC, 128], BF16)
    sqt = sb("sqt", [128, 512], F32)
    stats = sb("stats", [1, 3, TC], F32)
    MEMb = sb("MEMb", [128, TC], F32)
    INVb = sb("INVb", [128, TC], F32)
    silu_t = sb("silu_t", [128, 512], F32)
    T8 = sb("T8", [128, 128], FP8)
    TB = sb("TB", [128, 128], BF16)
    RED4 = sb("RED4", [128, 4], F32)
    SC4 = sb("SC4", [128, 4], F32)
    ones1 = sb("ones1", [1, 128], F32)
    ones_col = sb("ones_col", [128, 1], F32)
    identb = sb("identb", [128, 128], BF16)
    psr = lambda nm, sh: ctx.enter_context(nc.psum_tensor(nm, sh, F32))
    psA = psr("psA", [128, 512])
    psB = psr("psB", [128, 512])
    psC = psr("psC", [128, 512])
    psD = psr("psD", [128, 512])
    psT8 = ctx.enter_context(nc.psum_tensor("psT8", [128, 128], BF16))

    p = prog_ctx(nc, ctx)
    setup_const1(nc, ctx, p)

    dv = 0
    for dc in range(DC):
        dv = p.dma_on("sync", X[:, dc, :], xin[dc])
    dv_in = dv
    p.emit("gpsimd", lambda g: g.memset(ones1[:], 1.0))
    p.emit("gpsimd", lambda g: g.memset(ones_col[:], 1.0))
    p.emit("vector", lambda v: v.memset(identb[:], 0.0))
    p.vdrain()
    vmi = p.mark("vector")
    p.wait("gpsimd", "vector", vmi)
    p.emit("gpsimd", lambda g: g.affine_select(out=identb[:], in_=identb[:], compare_op=ALU.not_equal,
                                               fill=1.0, base=0, pattern=[[-1, 128]], channel_multiplier=1))
    gm0 = p.mark("gpsimd")
    p.wait_dma("scalar", dv_in)
    p.wait_dma("vector", dv_in)
    p.wait_dma("tensor", dv_in)
    p.wait("tensor", "gpsimd", gm0)
    p.wait("scalar", "gpsimd", gm0)
    vm_xl = emit_ln(p, X, XL, sqt, stats, MEMb, INVb, psA, psB, ones1, TC, TCS_C, ones_col)

    def mx_quant_tile(buf3, chunk_i, tslice_i):
        t0 = tslice_i * 128
        p.emit("tensor", lambda t: t.transpose(psT8[:], buf3[:, chunk_i, t0:t0 + 128], identb[:]))
        tm = p.mark("tensor")
        p.wait("vector", "tensor", tm)
        p.emit("vector", lambda v: v.reduce_max(RED4[:], psT8[:].rearrange("p (g e) -> p g e", e=32),
                                                axis=AXX, apply_absolute_value=True))
        p.vdrain()
        p.emit("vector", lambda v: v.tensor_scalar(out=SC4[:], in0=RED4[:], scalar1=1.0 / 448.0,
                                                   scalar2=1e-30, op0=ALU.mult, op1=ALU.max))
        p.vdrain()
        p.emit("vector", lambda v: v.reciprocal(RED4[:], SC4[:]))
        p.vdrain()
        p.emit("vector", lambda v: v.tensor_scalar(out=RED4[:], in0=RED4[:], scalar1=0.5,
                                                   scalar2=None, op0=ALU.mult))
        p.emit("vector", lambda v: v.tensor_scalar(out=SC4[:], in0=SC4[:], scalar1=2.0,
                                                   scalar2=None, op0=ALU.mult))
        p.vdrain()
        p.emit("vector", lambda v: v.tensor_tensor(
            T8[:].rearrange("p (g e) -> p g e", e=32), psT8[:].rearrange("p (g e) -> p g e", e=32),
            RED4[:, :, None].broadcast_to([128, 4, 32]), op=ALU.mult))
        p.vdrain()
        p.emit("vector", lambda v: v.tensor_tensor(
            TB[:].rearrange("p (g e) -> p g e", e=32), T8[:].rearrange("p (g e) -> p g e", e=32),
            SC4[:, :, None].broadcast_to([128, 4, 32]), op=ALU.mult))
        p.vdrain()
        vm = p.mark("vector")
        p.wait("tensor", "vector", vm)
        p.emit("tensor", lambda t: t.transpose(psT8[:], TB[:], identb[:]))
        tm = p.mark("tensor")
        p.wait("vector", "tensor", tm)
        p.emit("vector", lambda v: v.tensor_copy(buf3[:, chunk_i, t0:t0 + 128], psT8[:]))
        p.vdrain()
        vm = p.mark("vector")
        p.wait("tensor", "vector", vm)

    p.wait("tensor", "vector", vm_xl)
    for dc in range(DC):
        for tti in range(TC // 128):
            mx_quant_tile(XL, dc, tti)
    vm_xq = p.cnt["vector"]

    for h0 in (0, 512):
        base_t = p.cnt["tensor"]
        p.wait("tensor", "vector", p.cnt["vector"])
        for jt in range(NJT):
            p.wait("gpsimd", "tensor", base_t + 2 * (jt - 1) if jt >= 2 else base_t)
            dvg = p.dma_on("gpsimd", wbuf[:, 2 * (jt % 2), :, :], wgu_t[jt])
            dvu = p.dma_on("gpsimd", wbuf[:, 2 * (jt % 2) + 1, :, :], wgu_t[NJT + jt])
            p.wait_dma("tensor", dvu)
            for dc in range(DC):
                p.emit("tensor", lambda t, jt=jt, dc=dc, h0=h0: t.matmul(
                    psA[:, 0:512], wbuf[:, 2 * (jt % 2), dc, :], XL[:, dc, h0:h0 + 512],
                    start=(dc == 0), stop=(dc == DC - 1)))
            for dc in range(DC):
                p.emit("tensor", lambda t, jt=jt, dc=dc, h0=h0: t.matmul(
                    psC[:, 0:512], wbuf[:, 2 * (jt % 2) + 1, dc, :], XL[:, dc, h0:h0 + 512],
                    start=(dc == 0), stop=(dc == DC - 1)))
            tm = p.mark("tensor")
            p.wait("scalar", "tensor", tm)
            p.emit("scalar", lambda s: s.activation(out=silu_t[:, 0:512], in_=psA[:, 0:512], func=AF.Silu))
            sm = p.mark("scalar")
            p.wait("vector", "scalar", sm)
            p.emit("vector", lambda v, jt=jt: v.tensor_tensor(
                H[:, jt, :], silu_t[:, 0:512], psC[:, 0:512], op=ALU.mult))
            p.vdrain()
            vm = p.mark("vector")
            p.wait("tensor", "vector", vm)
            tm = p.mark("tensor")
        for hc in range(NJT):
            for tti in range(4):
                mx_quant_tile(H, hc, tti)
        vm_hq = p.cnt["vector"]
        base_t2 = p.cnt["tensor"]
        p.wait("tensor", "vector", vm_hq)
        for jt in range(DC):
            p.wait("gpsimd", "tensor", base_t2 + jt if jt >= 1 else base_t2)
            dvw = p.dma_on("gpsimd", wbuf[:].rearrange("p a c j -> p (a c) j"), wp_t[jt])
            p.wait_dma("tensor", dvw)
            for hc in range(NJT):
                p.emit("tensor", lambda t, jt=jt, hc=hc: t.matmul(
                    psA[:, 0:512], wbuf[:].rearrange("p a c j -> p (a c) j")[:, hc, :],
                    H[:, hc, :], start=(hc == 0), stop=(hc == NJT - 1)))
            tm = p.mark("tensor")
            p.wait("vector", "tensor", tm)
            p.emit("vector", lambda v, jt=jt, h0=h0: v.tensor_tensor(
                X[:, jt, h0:h0 + 512], X[:, jt, h0:h0 + 512], psA[:, 0:512], op=ALU.add))
            p.vdrain()
            vm = p.mark("vector")
            p.wait("tensor", "vector", vm)
    vm_fin = p.cnt["vector"]
    p.wait("sync", "vector", vm_fin)
    p.wait_dma("sync", "all")
    for dc in range(DC):
        dv = p.dma_on("sync", xout[dc], X[:, dc, :])
    p.wait_dma("gpsimd", "all")
    with ctx, nc.Block() as block:
        p.run_blocks(block)
    return nc


CHUNKS = [(0, 128), (128, 256), (384, 256), (640, 256), (896, 256)]
NCONV = 34


def ctxstats(nc, ctx):
    return ctx.enter_context(nc.sbuf_tensor("statsC", [1, 3, TT], F32))


def build_launch_C():
    from contextlib import ExitStack
    nc = bass.Bass("TRN2", target_bir_lowering=False)
    NJP = 67
    xin = nc.dram_tensor("xin", [DC, 128, TT], F32, kind="ExternalInput").ap()
    inpj_t = nc.dram_tensor("inpj_t", [NJP, 128, DC, 128], F32, kind="ExternalInput").ap()
    outp_t = nc.dram_tensor("outp_t", [DC, 128, 32, 128], F32, kind="ExternalInput").ap()
    convw = nc.dram_tensor("convw", [128, NCONV, KCONV], F32, kind="ExternalInput").ap()
    convb = nc.dram_tensor("convb", [128, NCONV], F32, kind="ExternalInput").ap()
    dtb = nc.dram_tensor("dtb", [64, 1], F32, kind="ExternalInput").ap()
    alog = nc.dram_tensor("alog", [64, 1], F32, kind="ExternalInput").ap()
    ddrow = nc.dram_tensor("ddrow", [1, 64], F32, kind="ExternalInput").ap()
    normw = nc.dram_tensor("normw", [1, INTER], F32, kind="ExternalInput").ap()
    masks = nc.dram_tensor("masks", [128, 2, 256], F32, kind="ExternalInput").ap()
    gate_tm = nc.dram_tensor("gate_tm", [NTT, 128, 32, 128], BF16, kind="Internal").ap()
    yn_fm = nc.dram_tensor("yn_fm", [32, 128, TC], BF16, kind="Internal").ap()
    xout = nc.dram_tensor("xout", [DC, 128, TC], F32, kind="ExternalOutput").ap()

    ctx = ExitStack()
    sb = lambda nm, sh, dt: ctx.enter_context(nc.sbuf_tensor(nm, sh, dt))
    BIG1 = sb("BIG1", [128, DC * TT], F32)
    X = BIG1[:].rearrange("p (c t) -> p c t", t=TT)
    HSTM = BIG1[:, 0:NTT * INTER // 2].bitcast(BF16).rearrange("p (n j) -> p n j", j=INTER)
    YNF = BIG1[:, 0:32 * TC // 2].bitcast(BF16).rearrange("p (c t) -> p c t", t=TC)
    XL = sb("XL", [128, DC, TT], BF16)
    BCfm = sb("BCfm", [128, 2, TT], BF16)
    PRE = sb("PRE", [128, KCONV - 1 + TT], BF16)
    CVS = sb("CVS", [128, TT], BF16)
    wbuf = sb("wbuf", [128, 2, DC, 128], BF16)
    WB2 = wbuf[:].rearrange("p b c j -> p (b c) j")
    sqt = sb("sqt", [128, 512], F32)
    MEMb = sb("MEMb", [128, TT], BF16)
    INVb = sb("INVb", [128, TT], BF16)
    SG = sb("SG", [128, 512], BF16)
    GTst = sb("GTst", [128, INTER], BF16)
    dtv = sb("dtv", [64, TT], F32)
    adt = sb("adt", [64, TT], F32)
    cum = sb("cum", [64, TT], F32)
    cum_tm = sb("cum_tm", [128, NTT, 64], F32)
    dtv_tm = sb("dtv_tm", [128, NTT, 64], F32)
    Ddt = sb("Ddt", [128, NTT, 64], F32)
    Db = sb("Db", [128, 64], F32)
    NW = sb("NW", [128, INTER], BF16)
    acol = sb("acol", [64, 1], F32)
    aneg = sb("aneg", [64, 1], F32)
    GmM = sb("GmM", [128, 2, 256], BF16)
    Mbf = sb("Mbf", [128, 2, 256], BF16)
    Btm = sb("Btm", [128, 2, 128], BF16)
    BD = sb("BD", [128, 128], BF16)
    DecT = sb("DecT", [128, 2, 64], F32)
    ST = XL[:].rearrange("p c t -> p (c t)")[:, 0:2 * NHEAD * 64].rearrange(
        "p (a h q) -> p a h q", h=NHEAD, q=64)
    YD = XL[:].rearrange("p c t -> p (c t)")[:, 2 * NHEAD * 64:4 * NHEAD * 64].rearrange(
        "p (a h q) -> p a h q", h=NHEAD, q=64)
    MSK = sb("MSK", [128, 2, 256], F32)
    tmp512 = sb("tmp512", [128, 512], F32)
    CVA = sb("CVA", [128, TT], F32)
    dtraw = sb("dtraw", [64, TT], F32)
    spt = CVA[0:64, :]
    Mex = CVA[:, 0:256]
    rv = sb("rv", [128, 4], F32)
    SdoTv = sb("SdoTv", [128, 64], F32)
    CR = sb("CR", [1, 256], F32)
    Xst = sb("Xst", [128, TC], F32)
    ident = sb("ident", [128, 128], F32)
    identb = sb("identb", [128, 128], BF16)
    CVW = sb("CVW", [128, NCONV, KCONV], F32)
    CVB = sb("CVB", [128, NCONV], F32)
    DTB = sb("DTB", [64, 1], F32)
    ones1b = sb("ones1b", [1, 128], BF16)
    ones1 = sb("ones1", [1, 128], F32)
    ones_col = sb("ones_col", [128, 1], F32)
    psr = lambda nm, sh: ctx.enter_context(nc.psum_tensor(nm, sh, F32))
    psA = psr("psA", [128, 512])
    psB = psr("psB", [128, 512])
    psC = psr("psC", [128, 256])
    psD = psr("psD", [128, 64])
    psTrb = ctx.enter_context(nc.psum_tensor("psTrb", [128, 128], BF16))
    psTrf = psr("psTrf", [128, 128])

    p = prog_ctx(nc, ctx)
    setup_const1(nc, ctx, p)

    dv = 0
    for dc in range(DC):
        dv = p.dma_on("sync", X[:, dc, :], xin[dc])
    dv_in = dv
    for dst, srcd in [(acol[:], alog), (CVW[:], convw), (CVB[:], convb),
                      (DTB[:], dtb), (NW[0:1, :], normw), (MSK[:], masks)]:
        dvc = p.dma_on("gpsimd", dst, srcd)
    p.emit("gpsimd", lambda g: g.memset(ones1[:], 1.0))
    p.emit("gpsimd", lambda g: g.memset(ones1b[:], 1.0))
    p.emit("gpsimd", lambda g: g.memset(ones_col[:], 1.0))
    p.emit("gpsimd", lambda g: g.memset(PRE[:, 0:KCONV - 1], 0.0))
    p.emit("vector", lambda v: v.memset(ident[:], 0.0))
    p.emit("vector", lambda v: v.memset(identb[:], 0.0))
    p.vdrain()
    vmi = p.mark("vector")
    p.wait("gpsimd", "vector", vmi)
    p.emit("gpsimd", lambda g: g.affine_select(out=ident[:], in_=ident[:], compare_op=ALU.not_equal,
                                               fill=1.0, base=0, pattern=[[-1, 128]], channel_multiplier=1))
    p.emit("gpsimd", lambda g: g.affine_select(out=identb[:], in_=identb[:], compare_op=ALU.not_equal,
                                               fill=1.0, base=0, pattern=[[-1, 128]], channel_multiplier=1))
    gm0 = p.mark("gpsimd")
    p.wait_dma("scalar", dv_in)
    p.wait_dma("vector", dv_in)
    p.wait_dma("tensor", dv_in)
    p.wait("tensor", "gpsimd", gm0)
    p.wait("scalar", "gpsimd", gm0)
    p.wait("vector", "gpsimd", gm0)
    stats = ctxstats(nc, ctx)
    vm_xl = emit_ln(p, X, XL, sqt, stats, MEMb, INVb, psA, psB, ones1, TT, TCS_T, ones_col)

    base_t = p.cnt["tensor"]
    p.wait("tensor", "vector", vm_xl)
    for jt in range(NJP):
        p.wait("gpsimd", "tensor", p.cnt["tensor"] if jt >= 1 else base_t)
        dvw = p.dma_on("gpsimd", wbuf[:, jt % 2, :, :], inpj_t[jt])
        p.wait_dma("tensor", dvw)
        for ci, (t0, tw) in enumerate(TCS_T):
            for dc in range(DC):
                p.emit("tensor", lambda t, jt=jt, dc=dc, t0=t0, tw=tw: t.matmul(
                    psA[:, 0:tw], wbuf[:, jt % 2, dc, :], XL[:, dc, t0:t0 + tw],
                    start=(dc == 0), stop=(dc == DC - 1)))
            tm = p.mark("tensor")
            if jt < 32:
                p.wait("scalar", "tensor", tm)
                p.emit("scalar", lambda s, tw=tw: s.activation(out=SG[:, 0:tw], in_=psA[:, 0:tw], func=AF.Silu))
                sm = p.mark("scalar")
                p.wait("tensor", "scalar", sm)
                for k in range(tw // 128):
                    p.emit("tensor", lambda t, k=k: t.transpose(psTrb[:], SG[:, k * 128:(k + 1) * 128], identb[:]))
                    tm2 = p.mark("tensor")
                    p.wait("vector", "tensor", tm2)
                    p.emit("vector", lambda v: v.tensor_copy(GTst[:, 0:128], psTrb[:]))
                    p.vdrain()
                    vm = p.mark("vector")
                    p.wait("sync", "vector", vm)
                    tti = (t0 // 128) + k
                    dvo = p.dma_on("sync", gate_tm[tti, :, jt, :], GTst[:, 0:128])
                    p.wait("tensor", "vector", vm)
                    p.wait_dma("vector", dvo)
            elif jt < 32 + NCONV:
                ct = jt - 32
                p.wait("vector", "tensor", tm)
                p.emit("vector", lambda v, t0=t0, tw=tw: v.tensor_copy(
                    PRE[:, KCONV - 1 + t0:KCONV - 1 + t0 + tw], psA[:, 0:tw]))
                p.vdrain()
                vm = p.mark("vector")
                p.wait("tensor", "vector", vm)
                if ci == len(TCS_T) - 1:
                    p.emit("vector", lambda v, ct=ct: v.tensor_scalar(
                        out=CVA[:, 0:TT], in0=PRE[:, 0:TT], scalar1=CVW[:, ct, 0:1],
                        scalar2=None, op0=ALU.mult))
                    p.vdrain()
                    for k in range(1, KCONV):
                        p.emit("vector", lambda v, ct=ct, k=k: v.scalar_tensor_tensor(
                            out=CVA[:, 0:TT], in0=PRE[:, k:k + TT], scalar=CVW[:, ct, k:k + 1],
                            in1=CVA[:, 0:TT], op0=ALU.mult, op1=ALU.add))
                        p.vdrain()
                    vm = p.mark("vector")
                    p.wait("scalar", "vector", vm)
                    if ct < 32:
                        p.emit("scalar", lambda s, ct=ct: s.activation(
                            out=CVS[:], in_=CVA[:, 0:TT], func=AF.Silu, bias=CVB[:, ct:ct + 1]))
                        sm = p.mark("scalar")
                        p.wait("tensor", "scalar", sm)
                        for k in range(NTT):
                            p.emit("tensor", lambda t, k=k: t.transpose(
                                psTrb[:], CVS[:, k * 128:(k + 1) * 128], identb[:]))
                            tm2 = p.mark("tensor")
                            p.wait("vector", "tensor", tm2)
                            p.emit("vector", lambda v, k=k, ct=ct: v.tensor_copy(
                                HSTM[:, k, ct * 128:(ct + 1) * 128], psTrb[:]))
                            p.vdrain()
                            vm = p.mark("vector")
                            p.wait("tensor", "vector", vm)
                        p.wait("scalar", "vector", vm)
                    else:
                        p.emit("scalar", lambda s, ct=ct: s.activation(
                            out=BCfm[:, ct - 32, :], in_=CVA[:, 0:TT], func=AF.Silu, bias=CVB[:, ct:ct + 1]))
                        sm = p.mark("scalar")
                        p.wait("vector", "scalar", sm)
            else:
                p.wait("vector", "tensor", tm)
                p.emit("vector", lambda v, t0=t0, tw=tw: v.tensor_copy(
                    dtraw[:, t0:t0 + tw], psA[0:64, 0:tw]))
                p.vdrain()
                vm = p.mark("vector")
                p.wait("tensor", "vector", vm)

    p.emit("vector", lambda v: v.tensor_scalar(out=dtraw[:], in0=dtraw[:], scalar1=DTB[:, 0:1],
                                               scalar2=None, op0=ALU.add))
    p.vdrain()
    vmx = p.mark("vector")
    p.wait("scalar", "vector", vmx)
    emit_softplus(p, dtv[:], dtraw[:], spt[:])
    p.emit("vector", lambda v: v.tensor_scalar(out=dtv[:], in0=dtv[:], scalar1=0.001,
                                               scalar2=100.0, op0=ALU.max, op1=ALU.min))
    p.vdrain()
    vm = p.mark("vector")
    p.wait("scalar", "vector", vm)
    p.emit("scalar", lambda s: s.activation(out=aneg[:], in_=acol[:], func=AF.Exp))
    p.emit("scalar", lambda s: s.drain())
    sm = p.mark("scalar")
    p.wait("vector", "scalar", sm)
    p.emit("vector", lambda v: v.tensor_scalar(out=aneg[:], in0=aneg[:], scalar1=-1.0,
                                               scalar2=None, op0=ALU.mult))
    p.vdrain()
    p.emit("vector", lambda v: v.tensor_scalar(out=adt[:], in0=dtv[:], scalar1=aneg[:, 0:1],
                                               scalar2=None, op0=ALU.mult))
    p.vdrain()
    for (s0, W) in CHUNKS:
        p.emit("vector", lambda v, s0=s0, W=W: v.tensor_tensor_scan(
            cum[:, s0:s0 + W], adt[:, s0:s0 + W], adt[:, s0:s0 + W], 0.0,
            op0=ALU.add, op1=ALU.bypass))
        p.vdrain()
    vm = p.mark("vector")
    p.wait("tensor", "vector", vm)
    for k in range(NTT):
        p.emit("tensor", lambda t, k=k: t.transpose(psTrf[0:128, 0:64], cum[:, k * 128:(k + 1) * 128],
                                                    ident[0:64, 0:64]))
        tm = p.mark("tensor")
        p.wait("vector", "tensor", tm)
        p.emit("vector", lambda v, k=k: v.tensor_copy(cum_tm[:, k, :], psTrf[0:128, 0:64]))
        p.vdrain()
        vm = p.mark("vector")
        p.wait("tensor", "vector", vm)
        p.emit("tensor", lambda t, k=k: t.transpose(psTrf[0:128, 0:64], dtv[:, k * 128:(k + 1) * 128],
                                                    ident[0:64, 0:64]))
        tm = p.mark("tensor")
        p.wait("vector", "tensor", tm)
        p.emit("vector", lambda v, k=k: v.tensor_copy(dtv_tm[:, k, :], psTrf[0:128, 0:64]))
        p.vdrain()
        vm = p.mark("vector")
        p.wait("tensor", "vector", vm)
    for k in range(NTT):
        p.emit("vector", lambda v, k=k: v.tensor_tensor(
            HSTM[:, k, :].rearrange("p (h q) -> p h q", q=64),
            HSTM[:, k, :].rearrange("p (h q) -> p h q", q=64),
            dtv_tm[:, k, :, None].broadcast_to([128, 64, 64]), op=ALU.mult))
    p.vdrain()
    p.emit("vector", lambda v: v.reciprocal(Ddt[:].rearrange("p n h -> p (n h)"),
                                            dtv_tm[:].rearrange("p n h -> p (n h)")))
    p.vdrain()
    vm = p.mark("vector")
    p.wait("tensor", "vector", vm)
    dvd = p.dma_on("sync", CR[:, 0:64], ddrow)
    p.wait_dma("tensor", dvd)
    p.emit("tensor", lambda t: t.matmul(psA[:, 0:64], ones1[:], CR[:, 0:64], start=True, stop=True))
    tm = p.mark("tensor")
    p.wait("vector", "tensor", tm)
    p.emit("vector", lambda v: v.tensor_copy(Db[:], psA[:, 0:64]))
    p.vdrain()
    for k in range(NTT):
        p.emit("vector", lambda v, k=k: v.tensor_tensor(Ddt[:, k, :], Ddt[:, k, :], Db[:], op=ALU.mult))
    p.vdrain()
    vm = p.mark("vector")
    p.wait("tensor", "vector", vm)
    for k in range(INTER // 512):
        p.emit("tensor", lambda t, k=k: t.matmul(psA[:, 0:512], ones1b[:], NW[0:1, k * 512:(k + 1) * 512],
                                                 start=True, stop=True))
        tm = p.mark("tensor")
        p.wait("vector", "tensor", tm)
        p.emit("vector", lambda v, k=k: v.tensor_copy(NW[:, k * 512:(k + 1) * 512], psA[:, 0:512]))
        p.vdrain()
        vm = p.mark("vector")
        p.wait("tensor", "vector", vm)

    for ci, (s0, W) in enumerate(CHUNKS):
        nlt = W // 128
        tile0 = s0 // 128
        for lt in range(nlt):
            p.emit("tensor", lambda t, s0=s0, W=W, lt=lt: t.matmul(
                psC[:, 0:W], BCfm[:, 0, s0 + lt * 128:s0 + (lt + 1) * 128], BCfm[:, 1, s0:s0 + W],
                start=True, stop=True))
            tm = p.mark("tensor")
            p.wait("vector", "tensor", tm)
            p.emit("vector", lambda v, lt=lt, W=W: v.tensor_tensor(
                GmM[:, lt, 0:W], psC[:, 0:W], MSK[:, lt, 0:W], op=ALU.mult))
            p.vdrain()
            vm = p.mark("vector")
            p.wait("tensor", "vector", vm)
            p.emit("tensor", lambda t, s0=s0, lt=lt: t.transpose(
                psTrb[:], BCfm[:, 0, s0 + lt * 128:s0 + (lt + 1) * 128], identb[:]))
            tm = p.mark("tensor")
            p.wait("vector", "tensor", tm)
            p.emit("vector", lambda v, lt=lt: v.tensor_copy(Btm[:, lt, :], psTrb[:]))
            p.vdrain()
            vm = p.mark("vector")
            p.wait("tensor", "vector", vm)
        p.emit("tensor", lambda t, s0=s0, W=W: t.transpose(
            psTrf[0:1, 0:64], cum[:, s0 + W - 1:s0 + W], ident[0:64, 0:64]))
        tmq = p.mark("tensor")
        p.wait("vector", "tensor", tmq)
        p.emit("vector", lambda v: v.tensor_copy(CR[:, 0:64], psTrf[0:1, 0:64]))
        p.vdrain()
        vmq = p.mark("vector")
        p.wait("tensor", "vector", vmq)
        p.emit("tensor", lambda t: t.matmul(
            psB[:, 0:64], ones1[:], CR[:, 0:64], start=True, stop=True))
        tm = p.mark("tensor")
        p.wait("vector", "tensor", tm)
        for lt in range(nlt):
            p.emit("vector", lambda v, lt=lt, tile0=tile0: v.tensor_tensor(
                DecT[:, lt, :], psB[:, 0:64], cum_tm[:, tile0 + lt, :], op=ALU.subtract))
        p.vdrain()
        vm = p.mark("vector")
        p.wait("scalar", "vector", vm)
        for lt in range(nlt):
            p.emit("scalar", lambda s, lt=lt: s.activation(out=DecT[:, lt, :], in_=DecT[:, lt, :], func=AF.Exp))
            p.emit("scalar", lambda s: s.drain())
        sm = p.mark("scalar")
        p.wait("vector", "scalar", sm)
        p.wait("tensor", "scalar", sm)
        for h in range(NHEAD):
            for lt in range(nlt):
                p.emit("vector", lambda v, lt=lt, h=h: v.tensor_scalar(
                    out=BD[:], in0=Btm[:, lt, :], scalar1=DecT[:, lt, h:h + 1],
                    scalar2=None, op0=ALU.mult))
                p.vdrain()
                vm = p.mark("vector")
                p.wait("tensor", "vector", vm)
                p.emit("tensor", lambda t, lt=lt, h=h, tile0=tile0, nlt=nlt: t.matmul(
                    psD[:], BD[:], HSTM[:, tile0 + lt, h * 64:(h + 1) * 64],
                    start=(lt == 0), stop=(lt == nlt - 1)))
                tm = p.mark("tensor")
                p.wait("vector", "tensor", tm)
            p.emit("vector", lambda v, ci=ci, h=h: v.tensor_copy(ST[:, ci % 2, h, :], psD[:]))
            p.vdrain()
            vm = p.mark("vector")
            p.wait("tensor", "vector", vm)
            if ci == 0:
                continue
            p.emit("tensor", lambda t, h=h, s0=s0, W=W: t.matmul(
                psB[:, 0:W], ident[0:64, h:h + 1].broadcast_to([64, 128]),
                cum[:, s0:s0 + W], start=True, stop=True))
            tm = p.mark("tensor")
            p.wait("vector", "tensor", tm)
            for lt in range(nlt):
                p.emit("vector", lambda v, lt=lt, h=h, tile0=tile0, W=W: v.tensor_scalar(
                    out=Mex[:, 0:W], in0=psB[:, 0:W], scalar1=cum_tm[:, tile0 + lt, h:h + 1],
                    scalar2=0.0, op0=ALU.subtract, op1=ALU.min))
                p.vdrain()
                vm = p.mark("vector")
                p.wait("scalar", "vector", vm)
                p.emit("scalar", lambda s, W=W: s.activation(out=tmp512[:, 0:W], in_=Mex[:, 0:W], func=AF.Exp))
                p.emit("scalar", lambda s: s.drain())
                sm = p.mark("scalar")
                p.wait("vector", "scalar", sm)
                p.emit("vector", lambda v, lt=lt, W=W: v.tensor_tensor(
                    Mbf[:, lt, 0:W], GmM[:, lt, 0:W], tmp512[:, 0:W], op=ALU.mult))
                p.vdrain()
            vm = p.mark("vector")
            p.wait("tensor", "vector", vm)
            for zt in range(nlt):
                for lt in range(nlt):
                    p.emit("tensor", lambda t, zt=zt, lt=lt, h=h, tile0=tile0, nlt=nlt: t.matmul(
                        psD[:], Mbf[:, lt, zt * 128:zt * 128 + 128], HSTM[:, tile0 + lt, h * 64:(h + 1) * 64],
                        start=(lt == 0), stop=(lt == nlt - 1)))
                tm = p.mark("tensor")
                p.wait("vector", "tensor", tm)
                p.emit("vector", lambda v, zt=zt, h=h: v.tensor_copy(YD[:, zt, h, :], psD[:]))
                p.vdrain()
                vm = p.mark("vector")
                p.wait("tensor", "vector", vm)
        if ci == 0:
            continue
        for zt in range(nlt):
            p.wait("scalar", "vector", p.cnt["vector"])
            p.emit("scalar", lambda s, tile0=tile0, zt=zt: s.activation(
                out=SdoTv[:], in_=cum_tm[:, tile0 + zt, :], func=AF.Exp))
            p.emit("scalar", lambda s: s.drain())
            sm = p.mark("scalar")
            p.wait("vector", "scalar", sm)
            for hg in range(8):
                p.wait("tensor", "vector", p.cnt["vector"])
                p.emit("tensor", lambda t, s0=s0, zt=zt, hg=hg, ci=ci: t.matmul(
                    psA[:, 0:512], BCfm[:, 1, s0 + zt * 128:s0 + (zt + 1) * 128],
                    ST[:, (ci - 1) % 2, hg * 8:(hg + 1) * 8, :].rearrange("p a b -> p (a b)"),
                    start=True, stop=True))
                tm = p.mark("tensor")
                p.wait("vector", "tensor", tm)
                p.emit("vector", lambda v, hg=hg: v.tensor_tensor(
                    tmp512[:, 0:512].rearrange("p (a b) -> p a b", b=64),
                    psA[:, 0:512].rearrange("p (a b) -> p a b", b=64),
                    SdoTv[:, hg * 8:(hg + 1) * 8, None].broadcast_to([128, 8, 64]), op=ALU.mult))
                p.vdrain()
                hsl = lambda: HSTM[:, tile0 + zt, hg * 512:(hg + 1) * 512]
                p.emit("vector", lambda v, zt=zt, hg=hg, tile0=tile0: v.tensor_tensor(
                    HSTM[:, tile0 + zt, hg * 512:(hg + 1) * 512].rearrange("p (a b) -> p a b", b=64),
                    HSTM[:, tile0 + zt, hg * 512:(hg + 1) * 512].rearrange("p (a b) -> p a b", b=64),
                    Ddt[:, tile0 + zt, hg * 8:(hg + 1) * 8][:, :, None].broadcast_to([128, 8, 64]),
                    op=ALU.mult))
                p.vdrain()
                p.emit("vector", lambda v, zt=zt, hg=hg, tile0=tile0: v.tensor_tensor(
                    HSTM[:, tile0 + zt, hg * 512:(hg + 1) * 512],
                    HSTM[:, tile0 + zt, hg * 512:(hg + 1) * 512], tmp512[:, 0:512], op=ALU.add))
                p.vdrain()
                p.emit("vector", lambda v, zt=zt, hg=hg: v.tensor_tensor(
                    HSTM[:, tile0 + zt, hg * 512:(hg + 1) * 512],
                    HSTM[:, tile0 + zt, hg * 512:(hg + 1) * 512],
                    YD[:, zt, hg * 8:(hg + 1) * 8, :].rearrange("p a b -> p (a b)"), op=ALU.add))
                p.vdrain()
                vm = p.mark("vector")
                p.wait("tensor", "vector", vm)

    for k in range(1, NTT):
        dvg = p.dma_on("sync", GTst[:], gate_tm[k].rearrange("p a b -> p (a b)"))
        p.wait_dma("vector", dvg)
        p.emit("vector", lambda v, k=k: v.tensor_tensor(HSTM[:, k, :], HSTM[:, k, :], GTst[:], op=ALU.mult))
        p.vdrain()
        vm = p.mark("vector")
        p.wait("scalar", "vector", vm)
        p.emit("scalar", lambda s, k=k: s.activation(out=GTst[:], in_=HSTM[:, k, :], func=AF.Square))
        p.emit("scalar", lambda s: s.drain())
        sm = p.mark("scalar")
        p.wait("vector", "scalar", sm)
        p.emit("vector", lambda v: v.reduce_sum(rv[:, 0:1], GTst[:], axis=AXX))
        p.vdrain()
        p.emit("vector", lambda v: v.tensor_scalar(out=rv[:, 1:2], in0=rv[:, 0:1],
                                                   scalar1=1.0 / INTER, scalar2=1e-5,
                                                   op0=ALU.mult, op1=ALU.add))
        p.vdrain()
        vm = p.mark("vector")
        p.wait("scalar", "vector", vm)
        p.emit("scalar", lambda s: s.activation(out=rv[:, 2:3], in_=rv[:, 1:2], func=AF.Sqrt))
        p.emit("scalar", lambda s: s.drain())
        sm = p.mark("scalar")
        p.wait("vector", "scalar", sm)
        p.emit("vector", lambda v: v.reciprocal(rv[:, 3:4], rv[:, 2:3]))
        p.vdrain()
        p.emit("vector", lambda v, k=k: v.tensor_scalar(out=HSTM[:, k, :], in0=HSTM[:, k, :],
                                                        scalar1=rv[:, 3:4], scalar2=None, op0=ALU.mult))
        p.vdrain()
        p.emit("vector", lambda v, k=k: v.tensor_tensor(HSTM[:, k, :], HSTM[:, k, :], NW[:], op=ALU.mult))
        p.vdrain()
        vm = p.mark("vector")
        p.wait("tensor", "vector", vm)
        for hc in range(32):
            p.emit("tensor", lambda t, k=k, hc=hc: t.transpose(
                psTrb[:], HSTM[:, k, hc * 128:(hc + 1) * 128], identb[:]))
            tm = p.mark("tensor")
            p.wait("vector", "tensor", tm)
            p.emit("vector", lambda v: v.tensor_copy(GTst[:, 0:128], psTrb[:]))
            p.vdrain()
            vm = p.mark("vector")
            p.wait("sync", "vector", vm)
            dvo = p.dma_on("sync", yn_fm[hc, :, (k - 1) * 128:k * 128], GTst[:, 0:128])
            p.wait("tensor", "vector", vm)
            p.wait_dma("vector", dvo)

    for hc in range(32):
        dvy = p.dma_on("sync", YNF[:, hc, :], yn_fm[hc])
    p.wait_dma("tensor", dvy)
    base_t3 = p.cnt["tensor"]
    for jt in range(DC):
        p.wait("gpsimd", "tensor", p.cnt["tensor"] if jt >= 1 else base_t3)
        dvw = p.dma_on("gpsimd", WB2, outp_t[jt])
        p.wait_dma("tensor", dvw)
        dvx = p.dma_on("sync", Xst[:], xin[jt][:, HALO:])
        p.wait_dma("vector", dvx)
        for tci, (t0, tw) in enumerate(TCS_C):
            for hc in range(32):
                p.emit("tensor", lambda t, jt=jt, hc=hc, t0=t0, tw=tw: t.matmul(
                    psA[:, 0:tw], WB2[:, hc, :], YNF[:, hc, t0:t0 + tw],
                    start=(hc == 0), stop=(hc == 31)))
            tm = p.mark("tensor")
            p.wait("vector", "tensor", tm)
            p.emit("vector", lambda v, t0=t0, tw=tw: v.tensor_tensor(
                Xst[:, t0:t0 + tw], Xst[:, t0:t0 + tw], psA[:, 0:tw], op=ALU.add))
            p.vdrain()
            vm = p.mark("vector")
            p.wait("tensor", "vector", vm)
        p.wait("sync", "vector", vm)
        dvo = p.dma_on("sync", xout[jt], Xst[:])
        p.wait_dma("vector", dvo)
    p.wait_dma("gpsimd", "all")
    with ctx, nc.Block() as block:
        p.run_blocks(block)
    return nc



def _slices_nohalo(x_fm):
    return [np.ascontiguousarray(x_fm[:, c * TC:(c + 1) * TC].reshape(DC, 128, TC))
            for c in range(8)]


def kernel(**inputs):
    f32 = np.float32
    x0 = np.asarray(inputs["L0_attn_x"], f32)
    x_fm = np.ascontiguousarray(x0.reshape(-1, D).T)

    wgu = np.asarray(inputs["L0_ffn_w_gate_up"], f32)
    wpr = np.asarray(inputs["L0_ffn_w_proj"], f32).reshape(8192, 2048)
    gw = np.asarray(inputs["L1_ffn_gate_w"], f32)
    uw = np.asarray(inputs["L1_ffn_up_w"], f32)
    dw = np.asarray(inputs["L1_ffn_down_w"], f32).reshape(8192, 2048)
    tern_full = np.ascontiguousarray(np.concatenate([wgu, wpr], 0))
    mx_full = np.ascontiguousarray(np.concatenate([gw, uw, dw], 0))
    NR = tern_full.shape[0] // 8
    ncQ = build_launch_Q(NR, NR)
    in_maps = [{"wt_in": np.ascontiguousarray(tern_full[i * NR:(i + 1) * NR]),
                "wm_in": np.ascontiguousarray(mx_full[i * NR:(i + 1) * NR])}
               for i in range(8)]
    resQ = run_bass_kernel_spmd(ncQ, in_maps, core_ids=list(range(8)))
    wt_q = np.concatenate([np.asarray(r["wt_out"]) for r in resQ.results], 0)
    wm_q = np.concatenate([np.asarray(r["wm_out"]) for r in resQ.results], 0)
    wqgu = wt_q[:16384]
    wqp = wt_q[16384:].reshape(2048, 8192)
    mqg, mqu, mqd = wm_q[:8192], wm_q[8192:16384], wm_q[16384:].reshape(2048, 8192)

    ncA = build_launch_A()
    wA = host_inputs_A(inputs)
    in_maps = [dict(wA, xin=s) for s in slice_cores_fm(x_fm, HALO)]
    resA = run_bass_kernel_spmd(ncA, in_maps, core_ids=list(range(8)))
    x1_fm = np.concatenate([np.asarray(r["xout"]).reshape(D, TC) for r in resA.results], 1)

    ncB = build_launch_B()
    wgu_t = tile_lhsT(wqgu)
    wp_t = tile_lhsT(wqp)
    in_maps = [{"xin": s, "wgu_t": wgu_t, "wp_t": wp_t} for s in _slices_nohalo(x1_fm)]
    resB = run_bass_kernel_spmd(ncB, in_maps, core_ids=list(range(8)))
    x2_fm = np.concatenate([np.asarray(r["xout"]).reshape(D, TC) for r in resB.results], 1)

    ncC = build_launch_C()
    ipw = np.asarray(inputs["L1_attn_in_proj_w"], f32)
    inpj_t = tile_lhsT(ipw, pad_to=67 * 128)
    outp_t = tile_lhsT(np.asarray(inputs["L1_attn_out_proj_w"], f32))
    cw = np.asarray(inputs["L1_attn_conv_w"], f32).reshape(34, 128, 4).transpose(1, 0, 2)
    cb = np.asarray(inputs["L1_attn_conv_b"], f32).reshape(34, 128).T
    m0 = (np.arange(256)[None, :] >= np.arange(128)[:, None]).astype(f32)
    m1 = (np.arange(256)[None, :] >= (128 + np.arange(128))[:, None]).astype(f32)
    masks = np.ascontiguousarray(np.stack([m0, m1], 1))
    wC = {"inpj_t": inpj_t, "outp_t": outp_t,
          "convw": np.ascontiguousarray(cw), "convb": np.ascontiguousarray(cb),
          "dtb": np.asarray(inputs["L1_attn_dt_bias"], f32).reshape(64, 1),
          "alog": np.asarray(inputs["L1_attn_A_log"], f32).reshape(64, 1),
          "ddrow": np.asarray(inputs["L1_attn_D"], f32).reshape(1, 64),
          "normw": np.asarray(inputs["L1_attn_norm_w"], f32).reshape(1, INTER),
          "masks": masks}
    in_maps = [dict(wC, xin=s) for s in slice_cores_fm(x2_fm, HALO)]
    resC = run_bass_kernel_spmd(ncC, in_maps, core_ids=list(range(8)))
    x3_fm = np.concatenate([np.asarray(r["xout"]).reshape(D, TC) for r in resC.results], 1)

    ncD = build_launch_D()
    wguD = tile_lhsT(np.ascontiguousarray(np.concatenate([mqg, mqu], 0)))
    wpD = tile_lhsT(mqd)
    in_maps = [{"xin": s, "wgu_t": wguD, "wp_t": wpD} for s in _slices_nohalo(x3_fm)]
    resD = run_bass_kernel_spmd(ncD, in_maps, core_ids=list(range(8)))
    x4_fm = np.concatenate([np.asarray(r["xout"]).reshape(D, TC) for r in resD.results], 1)

    return np.ascontiguousarray(x4_fm.T).reshape(B, TSEQ, D)


if __name__ == "__main__":
    print("module ok")
